# version 03
# baseline (speedup 1.0000x reference)
import numpy as np
import ml_dtypes
from contextlib import ExitStack
import concourse.bass as bass
import concourse.mybir as mybir
from concourse.bass_utils import run_bass_kernel_spmd

F32 = mybir.dt.float32
BF16 = mybir.dt.bfloat16
AF = mybir.ActivationFunctionType
ALU = mybir.AluOpType
AX = mybir.AxisListType
BF = ml_dtypes.bfloat16

HALO = 2560
OWN = 4096
R = HALO + OWN
D = 1024
NEG = -30000.0
P1_LO = 384
P2Q_LO, P2A_LO, P3_LO = 512, 1024, 1408
P4Q_LO, P4A_LO, P5_LO = 1536, 2048, 2560
HID = 2730
HID2P = 5632
HIDP = 2816
NPAIR = 11

ENGINES = ("sync", "tensor", "vector", "scalar", "gpsimd")


def _ranges(lo, hi, step=512):
    out = []
    a = lo
    while a < hi:
        out.append((a, min(a + step, hi)))
        a += step
    return out


class _Sem:
    __slots__ = ("h", "name", "count")

    def __init__(self, h, name):
        self.h = h
        self.name = name
        self.count = 0


class Prog:
    def __init__(self, nc):
        self.nc = nc
        self.ops = {e: [] for e in ENGINES}
        self._wm = {e: {} for e in ENGINES}

    def sem(self, es, name):
        return _Sem(es.enter_context(self.nc.semaphore(name)), name)

    def emit(self, eng, meth, *args, inc=None, incn=1, **kw):
        tok = None
        if inc is not None:
            inc.count += incn
            tok = (inc, inc.count)

        def fn(h, meth=meth, args=args, kw=kw, inc=inc, incn=incn):
            ins = getattr(h, meth)(*args, **kw)
            if inc is not None:
                ins.then_inc(inc.h, incn)
        self.ops[eng].append(fn)
        return tok

    def wait(self, eng, tok):
        if tok is None:
            return
        sem, val = tok
        wm = self._wm[eng]
        if wm.get(id(sem), -1) >= val:
            return
        wm[id(sem)] = val
        self.ops[eng].append(lambda h, s=sem.h, v=val: h.wait_ge(s, v))

    def waits(self, eng, toks):
        for t in toks:
            self.wait(eng, t)

    def replay(self):
        ops = self.ops
        with self.nc.Block() as block:
            block.sync(lambda h: [f(h) for f in ops["sync"]])
            block.tensor(lambda h: [f(h) for f in ops["tensor"]])
            block.vector(lambda h: [f(h) for f in ops["vector"]])
            block.scalar(lambda h: [f(h) for f in ops["scalar"]])
            block.gpsimd(lambda h: [f(h) for f in ops["gpsimd"]])


class Psum:
    def __init__(self, ap):
        self.ap = ap
        self.rel = []


def sap(t, off, pairs):
    ten = t[:] if not isinstance(t, bass.AP) else t
    pitch = ten.ap[0][0]
    return bass.AP(ten.tensor, off, [[pitch, ten.ap[0][1]]] + pairs)


class MK:

    def __init__(self, debug=False, phases=(1, 2, 3, 4, 5), parts="ABCDEVST",
                 use_g=True, ser_dma=False):
        self.debug = debug
        self.phases = phases
        self.parts = parts
        self.use_g = use_g
        self.ser_dma = ser_dma
        nc = bass.Bass("TRN2", target_bir_lowering=False, debug=False)
        self.nc = nc
        self.p = Prog(nc)
        self.inp = {}
        for name, shape, dt in [
            ("xin", [R, D], F32), ("Wqkv", [D, 3 * D], F32),
            ("WoutT", [D, D], F32), ("boutc", [128, 8], F32),
            ("fwqkv", [2, D, 1536], F32), ("fwo", [2, 512, D], F32),
            ("qsrep", [2, 128, 512], F32), ("ksrep", [2, 128, 512], F32),
            ("gcol", [2, 128, 8], F32), ("lnbcb", [2, 128, 8], BF16),
            ("fw1p", [2, D, HID2P], F32), ("b1pc", [2, 128, 44], F32),
            ("fw2p", [2, HIDP, D], F32), ("b2col", [2, 128, 8], F32),
            ("cq", [R, 64], F32), ("sq", [R, 64], F32),
            ("ck", [R, 64], F32), ("sk", [R, 64], F32),
            ("kvb2", [2, R], BF16), ("shiftc", [128, 1], F32),
            ("mtri_up", [128, 128], BF16), ("mtri_lo", [128, 128], BF16),
            ("mcaus", [128, 128], BF16), ("identb", [128, 128], BF16),
            ("identf", [128, 128], F32), ("onesZa", [2, 128], F32), ("onesZb", [2, 128], F32),
            ("e8", [128, 4 * 128], F32), ("onesbf", [128, 1], BF16),
            ("ones2b", [2, 512], BF16),
        ]:
            self.inp[name] = nc.dram_tensor(name, shape, dt, kind="ExternalInput").ap()

        self.y = nc.dram_tensor("y", [D, OWN], F32, kind="ExternalOutput").ap()
        spk = "ExternalOutput" if debug else "Internal"
        self.x1sp = nc.dram_tensor("x1sp", [D, R], F32, kind=spk).ap()
        self.statsRsp = nc.dram_tensor("statsRsp", [1, R], F32, kind=spk).ap()
        self.statsMsp = nc.dram_tensor("statsMsp", [1, R], F32, kind=spk).ap()
        self.xasp = nc.dram_tensor("xasp", [D, R], F32, kind=spk).ap()
        self.xlsp = nc.dram_tensor("xlsp", [D, R], F32, kind=spk).ap()
        self.xa2sp = nc.dram_tensor("xa2sp", [D, R], F32, kind=spk).ap()

    def op(self, eng, meth, *args, **kw):
        if eng == "gpsimd" and meth != "dma_start" and not self.use_g:
            eng = "vector"
        return self.p.emit(eng, meth, *args, inc=self.esem[eng], **kw)

    def dma(self, queue, out, in_, slot="c0", accum=None):
        s = self.dslots.get(slot)
        if s is None:
            s = self.p.sem(self.es, "sd_" + slot)
            self.dslots[slot] = s
        if s.count > 0:
            self.p.wait(queue, (s, s.count))
        if getattr(self, "ser_dma", False):
            for e2, s2 in self.esem.items():
                if s2.count > 0:
                    self.p.wait(queue, (s2, s2.count))
        kw = {} if accum is None else {"accum_op": accum}
        return self.p.emit(queue, "dma_start", incn=16, inc=s, out=out, in_=in_, **kw)

    def pacq(self, pb):
        self.p.waits("tensor", pb.rel)
        pb.rel = []

    def build(self):
        es = ExitStack()
        nc, p = self.nc, self.p
        with es:
            self.es = es

            def sb(name, shape, dt=F32):
                return es.enter_context(nc.sbuf_tensor(name, shape, dt))
            self.sb = sb

            self.esem = {e: p.sem(es, "se_" + e) for e in ENGINES if e not in ("sync",)}
            self.dslots = {}

            self.tident_b = sb("tident_b", [128, 128], BF16)
            self.tident_f = sb("tident_f", [128, 128], F32)
            self.tonesZa = sb("tonesZa", [2, 128], F32)
            self.tonesZb = sb("tonesZb", [2, 128], F32)
            self.consts = sb("consts", [128, 4], F32)
            self.tonesbf = sb("tonesbf", [128, 1], BF16)
            self.tshiftc = sb("tshiftc", [128, 1], F32)

            self.CT = {}
            for j, (key, tile) in enumerate([
                    ("identb", self.tident_b), ("identf", self.tident_f),
                    ("onesZa", self.tonesZa), ("onesZb", self.tonesZb),
                    ("shiftc", self.tshiftc), ("onesbf", self.tonesbf)]):
                self.CT[key] = self.dma("sync", tile[:], self.inp[key], slot=f"c{j % 4}")
            self.CT["cm8"] = self.op("vector", "memset", self.consts[:, 0:1], -8.0)


            def phase_scoped(fn, *a, **kw):
                barrier = {e: (s, s.count) for e, s in self.esem.items()}
                barrier.update({"d" + k: (s, s.count)
                                for k, s in self.dslots.items()})
                for e in ("tensor", "vector", "scalar", "gpsimd", "sync"):
                    for tok in barrier.values():
                        self.p.wait(e, tok)
                with ExitStack() as pes:
                    outer_sb = self.sb

                    def psb(name, shape, dt=F32):
                        return pes.enter_context(nc.sbuf_tensor(name, shape, dt))

                    def ppsm(name, shape, dt=F32):
                        return Psum(pes.enter_context(nc.psum_tensor(name, shape, dt)).ap())
                    self.sb = psb
                    self.psm = ppsm
                    try:
                        fn(*a, **kw)
                    finally:
                        self.sb = outer_sb

            out_toks = []
            if 1 in self.phases:
                phase_scoped(self.phase1)
            if 2 in self.phases:
                phase_scoped(self.ffn_qkv_attn, li=0, lo_q=P2Q_LO, lo_a=P2A_LO,
                             src_sp=self.x1sp, dst_sp=self.xasp,
                             ln=("L" not in self.parts))
            if 3 in self.phases:
                phase_scoped(self.ffn_mlp, li=0, lo_m=P3_LO, src_sp=self.xasp,
                             dst_sp=self.xlsp, final=False)
            if 4 in self.phases:
                phase_scoped(self.ffn_qkv_attn, li=1, lo_q=P4Q_LO, lo_a=P4A_LO,
                             src_sp=self.xlsp, dst_sp=self.xa2sp, ln=False)
            if 5 in self.phases:
                phase_scoped(self.ffn_mlp, li=1, lo_m=P5_LO, src_sp=self.xa2sp,
                             dst_sp=None, final=True)

            import os as _os
            for _ in range(int(_os.environ.get("NOPPAD", "0"))):
                self.p.emit("vector", "engine_nop")
            for t in self.final_toks:
                self.p.wait("gpsimd", t)
            p.replay()
        return self.nc

    final_toks = ()

    def phase1(self):
        p, op, dma, sb = self.p, self.op, self.dma, self.sb
        PBF = [self.psm(f"pb{i}", [128, 512], F32) for i in range(6)]
        PBB = [self.psm(f"pc{i}", [128, 1024], BF16) for i in range(2)]
        groups = _ranges(P1_LO, R)

        tmcaus = sb("p1_mcaus", [128, 128], BF16)
        tboutc = sb("p1_boutc", [128, 8], F32)
        t_mc = dma("sync", tmcaus[:], self.inp["mcaus"], slot="c0")
        t_bo = dma("sync", tboutc[:], self.inp["boutc"], slot="c1")
        wq = sb("p1_wq", [128, 8 * 3072], BF16)
        wo = sb("p1_wo", [128, 8 * 1024], BF16)
        twq = dma("gpsimd", wq[:].rearrange("p (g n) -> p g n", g=8),
                  self.inp["Wqkv"].rearrange("(g p) n -> p g n", p=128), slot="w0")
        two = dma("gpsimd", wo[:].rearrange("p (g n) -> p g n", g=8),
                  self.inp["WoutT"].rearrange("(g p) n -> p g n", p=128), slot="w1")

        def wq_ap(kc, mlo, w=128):
            return sap(wq, kc * 3072 + mlo, [[1, w]])

        def wo_ap(kc, mlo, w=128):
            return sap(wo, kc * 1024 + mlo, [[1, w]])

        xtm = [sb(f"p1_xtm{i}", [128, 1024], F32) for i in range(2)]
        xhtm = [sb(f"p1_xh{i}", [128, 1024], BF16) for i in range(2)]
        vtm = [sb(f"p1_v{i}", [128, 1024], BF16) for i in range(2)]
        x1c = [sb(f"p1_x1c{i}", [128, 512], F32) for i in range(2)]
        x1bf = [sb(f"p1_x1bf{i}", [128, 512], BF16) for i in range(2)]
        sq1 = [sb(f"p1_sq1{i}", [128, 512], BF16) for i in range(2)]
        xT = sb("p1_xT", [128, 8 * 512], F32)
        xhT = sb("p1_xhT", [128, 8 * 512], BF16)
        qT = sb("p1_qT", [128, 8 * 512], BF16)
        kT = sb("p1_kT", [128, 8 * 512], BF16)
        obuf = sb("p1_o", [128, 8 * 512], BF16)
        stt = sb("p1_st", [128, 24], F32)
        st2 = sb("p1_st2", [128, 16], F32)
        pbuf = sb("p1_pb", [128, 128], F32)
        pbuf2 = sb("p1_pb2", [128, 128], BF16)
        pbuf3 = sb("p1_pb3", [128, 128], BF16)
        accq = sb("p1_acc", [128, 1], F32)
        biasq = sb("p1_biasq", [128, 1], F32)
        rows = sb("p1_rows", [1, 8 * 512], F32)

        guard = {k: [] for k in ("xT", "xhT", "qT", "kT", "o", "st", "rows", "v0",
                                 "v1", "xtm0", "xtm1", "xh0", "xh1", "x1c0", "x1c1")}

        for gi, (a, b) in enumerate(groups):
            W = b - a
            nt = W // 128

            def load_tile(i):
                par = i % 2
                p.waits("sync", guard[f"xtm{par}"])
                return dma("sync", xtm[par][:],
                           self.inp["xin"][a + i * 128:a + (i + 1) * 128, :],
                           slot=f"x{par}")
            tx = {}
            for i in range(min(2, nt)):
                tx[i] = load_tile(i)

            xh_rdy, xT_rdy = [], []
            for i in range(nt if "A" in self.parts else 0):
                par = i % 2
                xtile = xtm[par][:]
                p.waits("vector", [tx[i]] + guard["st"])
                t1 = op("vector", "bn_stats", stt[:, 0:6], xtile[:, 0:512])
                t2 = op("vector", "bn_stats", stt[:, 6:12], xtile[:, 512:1024])
                p.waits("vector", [t1, t2])
                t3 = op("vector", "bn_aggr", st2[:, 0:2], stt[:, 0:12])
                p.wait("vector", t3)
                t4 = op("vector", "tensor_scalar", st2[:, 2:3], st2[:, 1:2],
                        1.0, 1e-5, ALU.mult, ALU.add)
                p.wait("scalar", t4)
                t5 = op("scalar", "activation", st2[:, 3:4], st2[:, 2:3], AF.Sqrt)
                p.wait("vector", t5)
                t6 = op("vector", "reciprocal", st2[:, 4:5], st2[:, 3:4])
                p.wait("vector", t6)
                t6 = self.newton(st2[:, 5:6], st2[:, 2:3], st2[:, 4:5])
                p.wait("vector", t6)
                t8 = op("vector", "tensor_scalar", st2[:, 6:7], st2[:, 0:1],
                        st2[:, 4:5], -1.0, ALU.mult, ALU.mult)
                p.waits("vector", [t6, t8] + guard[f"xh{par}"])
                t9 = op("vector", "tensor_scalar", xhtm[par][:], xtile,
                        st2[:, 4:5], st2[:, 6:7], ALU.mult, ALU.add)
                guard["st"] = [t9]

                pf = PBF[4]
                self.pacq(pf)
                p.waits("tensor", [tx[i], self.CT["identf"], self.CT["identb"]] + guard["xT"])
                for half in range(2):
                    for c4 in range(4):
                        c2 = half * 4 + c4
                        tt_f32 = op("tensor", "transpose",
                                    pf.ap[:, c4 * 128:(c4 + 1) * 128],
                                    xtile[:, c2 * 128:(c2 + 1) * 128],
                                    self.tident_f[:])
                    p.wait("vector", tt_f32)
                    tev = op("vector", "tensor_copy",
                             sap(xT, half * 4 * 512 + i * 128, [[512, 4], [1, 128]]),
                             pf.ap[:].rearrange("p (g w) -> p g w", g=4))
                    pf.rel.append(tev)
                    xT_rdy.append(tev)
                    if half == 0:
                        p.wait("tensor", tev)
                pbb_ = PBB[0]
                self.pacq(pbb_)
                p.waits("tensor", [t9] + guard["xhT"])
                for c2 in range(8):
                    tt = op("tensor", "transpose", pbb_.ap[:, c2 * 128:(c2 + 1) * 128],
                            xhtm[par][:, c2 * 128:(c2 + 1) * 128], self.tident_b[:])
                p.wait("vector", tt)
                tev2 = op("vector", "tensor_copy",
                          sap(xhT, i * 128, [[512, 8], [1, 128]]),
                          pbb_.ap[:].rearrange("p (g w) -> p g w", g=8))
                pbb_.rel.append(tev2)
                xh_rdy.append(tev2)
                guard[f"xtm{par}"] = [tt_f32, t1, t2, t9]
                guard[f"xh{par}"] = [tt]
                if i + 2 < nt:
                    tx[i + 2] = load_tile(i + 2)

            guard["xT"] = []
            guard["xhT"] = []

            if "B" not in self.parts:
                continue
            p.waits("tensor", xh_rdy + [twq] + guard["qT"] + guard["kT"])
            guard["qT"], guard["kT"] = [], []
            qk_rdy = []
            for which, dst in ((0, qT), (1, kT)):
                for mc in range(8):
                    pf = PBF[mc % 2]
                    self.pacq(pf)
                    for kc in range(8):
                        tmm = op("tensor", "matmul", pf.ap[:, 0:W],
                                 wq_ap(kc, which * 1024 + mc * 128),
                                 sap(xhT, kc * 512, [[1, W]]),
                                 start=(kc == 0), stop=(kc == 7))
                    p.wait("vector", tmm)
                    tev = op("vector", "tensor_copy", sap(dst, mc * 512, [[1, W]]),
                             pf.ap[:, 0:W])
                    pf.rel.append(tev)
                    qk_rdy.append(tev)

            o_rdy = []
            for i in range(nt if "C" in self.parts else 0):
                par = i % 2
                v_rdy = []
                p.waits("tensor", guard[f"v{par}"])
                for sl in range(2 if "V" in self.parts else 0):
                    pf = PBF[2 + sl]
                    self.pacq(pf)
                    for kc in range(8):
                        tmm = op("tensor", "matmul", pf.ap[:],
                                 sap(xhT, kc * 512 + i * 128, [[1, 128]]),
                                 wq_ap(kc, 2048 + sl * 512, 512),
                                 start=(kc == 0), stop=(kc == 7))
                    p.wait("vector", tmm)
                    tev = op("vector", "tensor_copy",
                             vtm[par][:, sl * 512:(sl + 1) * 512], pf.ap[:])
                    pf.rel.append(tev)
                    v_rdy.append(tev)

                if "S" not in self.parts:
                    guard[f"v{par}"] = v_rdy
                    continue
                pf = PBF[4]
                self.pacq(pf)
                p.waits("tensor", qk_rdy)
                for kc in range(8):
                    tmm = op("tensor", "matmul", pf.ap[:, 0:128],
                             sap(qT, kc * 512 + i * 128, [[1, 128]]),
                             sap(kT, kc * 512 + i * 128, [[1, 128]]),
                             start=(kc == 0), stop=False)
                p.waits("tensor", [t_mc])
                tmm = op("tensor", "matmul", pf.ap[:, 0:128], self.tident_b[:],
                         tmcaus[:], start=False, stop=True)
                p.wait("vector", tmm)
                if "1" in self.parts:
                    tev0 = op("vector", "tensor_copy", pbuf[:], pf.ap[:, 0:128])
                    pf.rel.append(tev0)
                    guard[f"v{par}"] = v_rdy
                    continue
                tmx = op("vector", "reduce_max", st2[:, 8:9], pf.ap[:, 0:128],
                         AX.X, negate=True)
                if "2" in self.parts:
                    p.wait("vector", tmx)
                    tev0 = op("vector", "tensor_copy", pbuf[:], pf.ap[:, 0:128])
                    pf.rel.append(tev0)
                    guard[f"v{par}"] = v_rdy
                    continue
                p.wait("vector", tmx)
                tsc = op("vector", "tensor_scalar", biasq[:], st2[:, 8:9],
                         1.0 / 32.0, None, ALU.mult)
                p.waits("scalar", [tsc, tmm])
                tex = op("scalar", "activation", pbuf[:], pf.ap[:, 0:128],
                         AF.Exp, bias=biasq[:], scale=1.0 / 32.0,
                         accum_out=None if "3" in self.parts else accq[:])
                if "3" in self.parts or "4" in self.parts:
                    pf.rel.append(tex)
                    guard[f"v{par}"] = v_rdy
                    continue
                pf.rel.append(tex)
                p.wait("vector", tex)
                td1 = op("vector", "tensor_scalar", st2[:, 11:12], accq[:],
                         1.0, 1e-6, ALU.mult, ALU.add)
                if "5" in self.parts:
                    guard[f"v{par}"] = v_rdy
                    continue
                p.wait("vector", td1)
                td2 = op("vector", "reciprocal", st2[:, 12:13], st2[:, 11:12])
                p.wait("vector", td2)
                if "6" in self.parts:
                    guard[f"v{par}"] = v_rdy
                    continue
                td3 = op("vector", "tensor_tensor", pbuf2[:], pbuf[:],
                         st2[:, 12:13].broadcast_to([128, 128]), ALU.mult)
                pbb_ = PBB[1]
                if "T" not in self.parts:
                    guard[f"v{par}"] = v_rdy
                    continue
                self.pacq(pbb_)
                p.waits("tensor", [td3])
                tpt = op("tensor", "transpose", pbb_.ap[:, 0:128], pbuf2[:],
                         self.tident_b[:])
                p.wait("vector", tpt)
                tpe = op("vector", "tensor_copy", pbuf3[:], pbb_.ap[:, 0:128])
                pbb_.rel.append(tpe)
                p.waits("tensor", [tpe] + v_rdy + guard["o"])
                for c2 in range(8):
                    pfo = PBF[5] if c2 % 2 == 0 else PBF[2]
                    self.pacq(pfo)
                    tav = op("tensor", "matmul", pfo.ap[:, 0:128],
                             vtm[par][:, c2 * 128:(c2 + 1) * 128],
                             pbuf3[:], start=True, stop=True)
                    p.wait("vector", tav)
                    tevo = op("vector", "tensor_copy",
                              sap(obuf, c2 * 512 + i * 128, [[1, 128]]),
                              pfo.ap[:, 0:128])
                    pfo.rel.append(tevo)
                    o_rdy.append(tevo)
                guard[f"v{par}"] = [tevo]
            guard["o"] = []

            if "D" not in self.parts:
                continue
            p.waits("tensor", o_rdy + [two] + xT_rdy)
            pfa, pfb = PBF[2], PBF[3]
            self.pacq(pfa)
            self.pacq(pfb)
            sp_toks = []
            stat_mm = None
            for mc in range(8):
                par = mc % 2
                pf = PBF[par]
                self.pacq(pf)
                for kc in range(8):
                    tmm = op("tensor", "matmul", pf.ap[:, 0:W], wo_ap(kc, mc * 128),
                             sap(obuf, kc * 512, [[1, W]]),
                             start=(kc == 0), stop=(kc == 7))
                p.waits("vector", [tmm] + guard[f"x1c{par}"])
                p.waits("vector", [t_bo])
                trs = op("vector", "scalar_tensor_tensor", x1c[par][:, 0:W],
                         pf.ap[:, 0:W], tboutc[:, mc:mc + 1],
                         sap(xT, mc * 512, [[1, W]]), ALU.add, ALU.add)
                pf.rel.append(trs)
                p.waits("gpsimd", [trs])
                tsp = dma("gpsimd",
                          self.x1sp[mc * 128:(mc + 1) * 128, a:b], x1c[par][:, 0:W],
                          slot=f"sp{mc % 4}")
                sp_toks.append(tsp)
                p.wait("vector", trs)
                tc1 = op("vector", "tensor_copy", x1bf[par][:, 0:W], x1c[par][:, 0:W])
                p.wait("scalar", tc1)
                tc2 = op("scalar", "activation", sq1[par][:, 0:W], x1bf[par][:, 0:W],
                         AF.Square)
                p.waits("tensor", [tc1, tc2, self.CT["onesbf"]])
                if "E" not in self.parts:
                    guard[f"x1c{par}"] = [tc1, tsp]
                    continue
                ts1 = op("tensor", "matmul", pfa.ap[0:1, 0:W], self.tonesbf[:],
                         x1bf[par][:, 0:W], start=(mc == 0), stop=(mc == 7))
                ts2 = op("tensor", "matmul", pfb.ap[0:1, 0:W], self.tonesbf[:],
                         sq1[par][:, 0:W], start=(mc == 0), stop=(mc == 7))
                stat_mm = ts2
                guard[f"x1c{par}"] = [tc1, tsp, ts1]
            guard["xT"] = [trs]
            guard["xhT"] = [tmm]
            guard["o"] = [tmm]
            guard["qT"] = [tmm]
            guard["kT"] = [tmm]

            self.x1_toks = getattr(self, "x1_toks", []) + sp_toks
            if "E" not in self.parts:
                guard["xT"] = [trs]
                guard["xhT"] = [tmm]
                guard["o"] = [tmm]
                guard["qT"] = [tmm]
                guard["kT"] = [tmm]
                continue
            sg = lambda j: rows[0:1, j * 512:j * 512 + W]
            p.waits("scalar", [stat_mm] + guard["rows"])
            te1 = op("scalar", "copy", sg(0), pfa.ap[0:1, 0:W])
            te2 = op("scalar", "copy", sg(1), pfb.ap[0:1, 0:W])
            pfa.rel.append(te1)
            pfb.rel.append(te2)
            p.waits("vector", [te1, te2])
            r1 = op("vector", "tensor_scalar", sg(2), sg(0), 1.0 / 1024.0, None, ALU.mult)
            r2 = op("vector", "tensor_scalar", sg(3), sg(1), 1.0 / 1024.0, None, ALU.mult)
            p.waits("vector", [r1, r2])
            r3 = op("vector", "tensor_tensor", sg(7), sg(2), sg(2), ALU.mult)
            p.wait("vector", r3)
            r4 = op("vector", "tensor_tensor", sg(4), sg(3), sg(7), ALU.subtract)
            p.wait("vector", r4)
            r5 = op("vector", "tensor_scalar", sg(4), sg(4), 1.0, 1e-5, ALU.mult, ALU.add)
            p.wait("scalar", r5)
            r6 = op("scalar", "activation", sg(5), sg(4), AF.Sqrt)
            p.wait("vector", r6)
            r7 = op("vector", "reciprocal", sg(6), sg(5))
            p.wait("vector", r7)
            r8 = self.newton(sg(7), sg(4), sg(6))
            p.wait("vector", r8)
            r10 = op("vector", "tensor_tensor", sg(2), sg(2), sg(6), ALU.mult)
            p.wait("vector", r10)
            r11 = op("vector", "tensor_scalar", sg(2), sg(2), -1.0, None, ALU.mult)
            p.waits("gpsimd", [r8, r11])
            tspr = dma("gpsimd", self.statsRsp[0:1, a:b], sg(6), slot="str")
            tspm = dma("gpsimd", self.statsMsp[0:1, a:b], sg(2), slot="stm")
            guard["rows"] = [tspr, tspm]
            self.x1_toks += [tspr, tspm]
        self.final_toks = list(getattr(self, "x1_toks", []))

    def newton(self, tmp, vpe, r):
        p, op = self.p, self.op
        t1 = op("vector", "tensor_tensor", tmp, vpe, r, ALU.mult)
        p.wait("vector", t1)
        t2 = op("vector", "tensor_tensor", tmp, tmp, r, ALU.mult)
        p.wait("vector", t2)
        t3 = op("vector", "tensor_scalar", tmp, tmp, -0.5, 1.5, ALU.mult, ALU.add)
        p.wait("vector", t3)
        t4 = op("vector", "tensor_tensor", r, r, tmp, ALU.mult)
        return t4

    def ffn_qkv_attn(self, li, lo_q, lo_a, src_sp, dst_sp, ln):
        p, op, dma, sb = self.p, self.op, self.dma, self.sb
        nc = self.nc
        pfx = f"f{li}_"

        PW0 = self.psm(pfx + "PW0", [128, 1024], F32)
        PW1 = self.psm(pfx + "PW1", [128, 1024], F32)
        SCa, SCb = Psum(PW0.ap[:, 0:512]), Psum(PW0.ap[:, 512:1024])
        AVa, AVb = Psum(PW1.ap[:, 0:512]), Psum(PW1.ap[:, 512:1024])
        PV = [self.psm(pfx + f"PV{i}", [128, 512], F32) for i in range(3)]
        PT = self.psm(pfx + "PT", [128, 1024], BF16)

        def pacq(*pbs):
            for pb in pbs:
                p.waits("tensor", pb.rel)
                pb.rel = []

        tkvb2 = sb(pfx + "kvb2", [2, R], BF16)
        tmtri_up = sb(pfx + "mtu", [128, 128], BF16)
        tmtri_lo = sb(pfx + "mtl", [128, 128], BF16)
        te8 = sb(pfx + "e8", [128, 4 * 128], F32)
        tones2b = sb(pfx + "o2b", [2, 512], BF16)
        t_kvb = dma("sync", tkvb2[:], self.inp["kvb2"], slot="c0")
        t_mtu = dma("sync", tmtri_up[:], self.inp["mtri_up"], slot="c1")
        t_mtl = dma("sync", tmtri_lo[:], self.inp["mtri_lo"], slot="c2")
        t_e8 = dma("sync", te8[:], self.inp["e8"], slot="c3")
        t_o2b = dma("sync", tones2b[:], self.inp["ones2b"], slot="c0")
        wqkv = sb(pfx + "wqkv", [128, 8 * 1536], BF16)
        wo = sb(pfx + "wo", [128, 4 * 1024], BF16)
        twq = dma("gpsimd", wqkv[:].rearrange("p (g n) -> p g n", g=8),
                  self.inp["fwqkv"][li].rearrange("(g p) n -> p g n", p=128), slot="w0")
        two = dma("gpsimd", wo[:].rearrange("p (g n) -> p g n", g=4),
                  self.inp["fwo"][li].rearrange("(g p) n -> p g n", p=128), slot="w1")
        qsr = sb(pfx + "qsr", [128, 512], BF16)
        ksr = sb(pfx + "ksr", [128, 512], BF16)
        tqs = dma("gpsimd", qsr[:], self.inp["qsrep"][li], slot="wq2")
        tks = dma("gpsimd", ksr[:], self.inp["ksrep"][li], slot="wk2")

        x1rd = [sb(pfx + f"x1rd{i}", [128, 8 * 513], F32) for i in range(2)]
        XFT = [sb(pfx + f"XFT{i}", [128, 8 * 513], BF16) for i in range(2)]
        xfb = [sb(pfx + f"xfb{i}", [128, 8 * 513], F32) for i in range(2)] if ln else x1rd
        stats2 = [sb(pfx + f"sts{i}", [2, 513], F32) for i in range(2)]
        tabs = [sb(pfx + f"tab{i}", [128, 4 * 4 * 64], F32) for i in range(2)]
        qTw = [sb(pfx + f"qTw{i}", [128, 4 * 512], BF16) for i in range(2)]
        kTw = [sb(pfx + f"kTw{i}", [128, 4 * 512], BF16) for i in range(2)]
        vwin = [sb(pfx + f"vw{i}", [128, 4 * 8 * 72], BF16) for i in range(2)]
        rsc = sb(pfx + "rsc", [128, 6 * 512], F32)
        nrm = sb(pfx + "nrm", [128, 48], F32)
        qp = sb(pfx + "qp", [128, 512], BF16)
        kp = sb(pfx + "kp", [128, 512], BF16)
        pbuf = sb(pfx + "pbuf", [128, 4 * 640], BF16)
        denr = sb(pfx + "denr", [128, 512], F32)
        rbsb = sb(pfx + "rbsb", [128, 512], F32)
        rbs2 = sb(pfx + "rbs2", [128, 513], F32) if ln else None
        mrsb = sb(pfx + "mrsb", [128, 513], F32) if ln else None
        obuf = sb(pfx + "obuf", [128, 4 * 512], BF16)
        ostg = sb(pfx + "ostg", [128, 512], BF16)
        xao = [sb(pfx + f"xao{i}", [128, 512], F32) for i in range(2)]

        GG = {}

        def gget(k):
            return GG.setdefault(k, [])

        vz = []
        for i in range(2):
            z = op("vector", "memset", vwin[i][:], 0)
            p.wait("vector", z)
            for h in range(8):
                z = op("vector", "memset",
                       sap(vwin[i], h * 72 + 64 + h, [[576, 4], [1, 1]]), 1.0)
            vz.append(z)
        zd = op("vector", "memset", denr[:], 0)
        p.waits("tensor", vz + [zd])

        def b3(t, off, reps, w):
            ten = t[:]
            return bass.AP(ten.tensor, off, [[ten.ap[0][0], 128], [0, reps], [1, w]])

        def load_group(g, par):
            toks = {}
            p.waits("sync", gget(f"x1rd{par}"))
            toks["x"] = dma("sync",
                            x1rd[par][:].rearrange("p (gg w) -> p gg w", gg=8),
                            src_sp[:, g - 1:g + 512].rearrange("(gg p) w -> p gg w", p=128),
                            slot=f"x{par}")
            if ln:
                p.waits("sync", gget(f"sts{par}"))
                toks["sr"] = dma("sync", stats2[par][0:1, :],
                                 self.statsRsp[0:1, g - 1:g + 512], slot=f"sr{par}")
                toks["sm"] = dma("sync", stats2[par][1:2, :],
                                 self.statsMsp[0:1, g - 1:g + 512], slot=f"sm{par}")
            p.waits("sync", gget(f"tab{par}"))
            for ti, key in enumerate(("cq", "sq", "ck", "sk")):
                toks[key] = dma("sync",
                                bass.AP(tabs[par][:].tensor, ti * 4 * 64,
                                        [[4 * 4 * 64, 128], [64, 4], [1, 64]]),
                                self.inp[key][g:g + 512, :].rearrange("(c p) d -> p c d", p=128),
                                slot=f"tb{par}")
            return toks

        def qkv_group(g, par, toks):
            kv_rdy = []
            a3 = [[513, 8], [1, 513]]
            if ln:
                pacq(PW0, PW1, SCa, SCb, AVa, AVb)
                p.waits("tensor", [toks["sr"], toks["sm"], self.CT["onesZa"], self.CT["onesZb"]])
                t_rb = op("tensor", "matmul", PW0.ap[:, 0:512], self.tonesZa[:],
                          stats2[par][:, 0:512], start=True, stop=True)
                t_rb2 = op("tensor", "matmul", PW0.ap[:, 512:513], self.tonesZa[:],
                           stats2[par][:, 512:513], start=True, stop=True)
                t_mb = op("tensor", "matmul", PW1.ap[:, 0:512], self.tonesZb[:],
                          stats2[par][:, 0:512], start=True, stop=True)
                t_mb2 = op("tensor", "matmul", PW1.ap[:, 512:513], self.tonesZb[:],
                           stats2[par][:, 512:513], start=True, stop=True)
                GG[f"sts_rd{par}"] = [t_rb, t_rb2, t_mb, t_mb2]
                p.waits("scalar", [t_rb, t_rb2, t_mb, t_mb2] + gget("rbs2"))
                e1 = op("scalar", "copy", rbs2[:, 0:513], PW0.ap[:, 0:513])
                e2 = op("scalar", "copy", mrsb[:, 0:513], PW1.ap[:, 0:513])
                for pb in (PW0, PW1, SCa, SCb, AVa, AVb):
                    pb.rel.append(e1 if pb in (PW0, SCa, SCb) else e2)
                p.waits("gpsimd", [toks["x"], e1] + gget(f"xfb{par}"))
                g1 = op("gpsimd", "tensor_tensor", sap(xfb[par], 0, a3),
                        sap(x1rd[par], 0, a3), b3(rbs2, 0, 8, 513), ALU.mult)
                GG[f"x1rd_rd{par}"] = [g1]
                p.waits("vector", [g1, e2])
                v1 = op("vector", "tensor_tensor", sap(xfb[par], 0, a3),
                        sap(xfb[par], 0, a3), b3(mrsb, 0, 8, 513), ALU.add)
                GG["rbs2"] = [g1]
                p.waits("scalar", [v1] + gget(f"XFT{par}"))
                s1 = op("scalar", "copy", sap(XFT[par], 0, a3), sap(xfb[par], 0, a3))
            else:
                p.waits("scalar", [toks["x"]] + gget(f"XFT{par}"))
                s1 = op("scalar", "copy", sap(XFT[par], 0, a3), sap(x1rd[par], 0, a3))
                GG[f"x1rd_rd{par}"] = [s1]
            xft_tok = s1
            if g == HALO:
                p.waits("vector", [s1, self.CT["shiftc"]])
                xft_tok = op("vector", "tensor_scalar",
                             sap(XFT[par], 4 * 513, [[513, 4], [1, 1]]),
                             sap(XFT[par], 4 * 513, [[513, 4], [1, 1]]),
                             self.tshiftc[:, 0:1], None, ALU.mult)
            GG[f"xft{par}"] = [xft_tok]

            for i in range(4):
                pacq(PV[0], PV[1], PV[2])
                p.waits("tensor", [xft_tok, twq])
                mm_t = {}
                for sl, pv in enumerate(PV):
                    for kc in range(8):
                        off = kc * 513 + i * 128 + (1 if kc < 4 else 0)
                        mm_t[sl] = op("tensor", "matmul", pv.ap[:, 0:512],
                                      sap(XFT[par], off, [[1, 128]]),
                                      sap(wqkv, kc * 1536 + sl * 512, [[1, 512]]),
                                      start=(kc == 0), stop=(kc == 7))
                p.waits("scalar", [mm_t[0], mm_t[1]] + gget("rsc15"))
                sq_q = op("scalar", "activation", rsc[:, 1 * 512:2 * 512], PV[0].ap[:, 0:512], AF.Square)
                sq_k = op("scalar", "activation", rsc[:, 4 * 512:5 * 512], PV[1].ap[:, 0:512], AF.Square)
                p.waits("vector", [sq_q, sq_k] + gget("nrm"))
                rd_q = op("vector", "tensor_reduce", nrm[:, 0:8],
                          sap(rsc, 1 * 512, [[64, 8], [1, 64]]), AX.X, ALU.add)
                p.wait("vector", rd_q)
                rd_k = op("vector", "tensor_reduce", nrm[:, 8:16],
                          sap(rsc, 4 * 512, [[64, 8], [1, 64]]), AX.X, ALU.add)
                p.waits("scalar", [rd_q, rd_k])
                n1 = op("scalar", "activation", nrm[:, 16:32], nrm[:, 0:16], AF.Sqrt)
                p.wait("vector", n1)
                n2 = op("vector", "tensor_scalar", nrm[:, 16:32], nrm[:, 16:32],
                        1e-12, None, ALU.max)
                p.wait("vector", n2)
                n3 = op("vector", "reciprocal", nrm[:, 32:48], nrm[:, 16:32])
                p.wait("vector", n3)
                n4 = self.newton(nrm[:, 16:32], nrm[:, 0:16], nrm[:, 32:48])
                p.waits("vector", [sq_q, sq_k, tqs, tks])
                v_qt = op("vector", "tensor_tensor", rsc[:, 0:512], PV[0].ap[:, 0:512],
                          qsr[:], ALU.mult)
                v_kt = op("vector", "tensor_tensor", rsc[:, 3 * 512:4 * 512],
                          PV[1].ap[:, 0:512], ksr[:], ALU.mult)
                PV[0].rel.append(v_qt)
                PV[1].rel.append(v_kt)
                p.waits("gpsimd", [v_qt, v_kt, rd_q, rd_k, toks["cq"], toks["sk"]])
                tb = lambda ti: b3(tabs[par], ti * 256 + i * 64, 8, 64)
                g_t1 = op("gpsimd", "tensor_tensor",
                          sap(rsc, 1 * 512, [[64, 8], [1, 64]]),
                          sap(rsc, 0, [[64, 8], [1, 64]]), tb(0), ALU.mult)
                sw_q = bass.AP(rsc[:].tensor, 0 + 32, [[6 * 512, 128], [64, 8], [-32, 2], [1, 32]])
                g_t2 = op("gpsimd", "tensor_tensor",
                          sap(rsc, 2 * 512, [[64, 8], [1, 64]]), sw_q,
                          bass.AP(tabs[par][:].tensor, 1 * 256 + i * 64,
                                  [[4 * 4 * 64, 128], [0, 8], [32, 2], [1, 32]]),
                          ALU.mult)
                g_k1 = op("gpsimd", "tensor_tensor",
                          sap(rsc, 4 * 512, [[64, 8], [1, 64]]),
                          sap(rsc, 3 * 512, [[64, 8], [1, 64]]), tb(2), ALU.mult)
                sw_k = bass.AP(rsc[:].tensor, 3 * 512 + 32, [[6 * 512, 128], [64, 8], [-32, 2], [1, 32]])
                g_k2 = op("gpsimd", "tensor_tensor",
                          sap(rsc, 5 * 512, [[64, 8], [1, 64]]), sw_k,
                          bass.AP(tabs[par][:].tensor, 3 * 256 + i * 64,
                                  [[4 * 4 * 64, 128], [0, 8], [32, 2], [1, 32]]),
                          ALU.mult)
                p.waits("vector", [g_t1, g_t2, g_k1, g_k2])
                v_q2 = op("vector", "tensor_add", rsc[:, 0:512],
                          rsc[:, 1 * 512:2 * 512], rsc[:, 2 * 512:3 * 512])
                v_k2 = op("vector", "tensor_add", rsc[:, 3 * 512:4 * 512],
                          rsc[:, 4 * 512:5 * 512], rsc[:, 5 * 512:6 * 512])
                p.waits("vector", [v_q2, n4] + gget("qp"))
                v_qp = op("vector", "tensor_tensor", qp[:].rearrange("p (h d) -> p h d", h=8),
                          sap(rsc, 0, [[64, 8], [1, 64]]),
                          bass.AP(nrm[:].tensor, 32, [[48, 128], [1, 8], [0, 64]]), ALU.mult)
                p.waits("vector", [v_k2] + gget("kp"))
                v_kp = op("vector", "tensor_tensor", kp[:].rearrange("p (h d) -> p h d", h=8),
                          sap(rsc, 3 * 512, [[64, 8], [1, 64]]),
                          bass.AP(nrm[:].tensor, 40, [[48, 128], [1, 8], [0, 64]]), ALU.mult)
                GG["rsc15"] = [v_q2, v_k2, v_qp, v_kp]
                GG["nrm"] = [v_qp, v_kp, n1, n2]
                GG.setdefault(f"tab_rd{par}", []).extend([g_t2, g_k2])
                p.waits("vector", [mm_t[2]] + gget(f"win_rd{par}"))
                v_ve = op("vector", "tensor_copy",
                          bass.AP(vwin[par][:].tensor, i * 576, [[4 * 576, 128], [72, 8], [1, 64]]),
                          PV[2].ap[:].rearrange("p (h d) -> p h d", h=8))
                PV[2].rel.append(v_ve)
                kv_rdy.append(v_ve)
                pacq(PT)
                p.waits("tensor", [v_qp, v_kp])
                for hp in range(4):
                    tq = op("tensor", "transpose", PT.ap[:, hp * 128:(hp + 1) * 128],
                            qp[:, hp * 128:(hp + 1) * 128], self.tident_b[:])
                for hp in range(4):
                    tk = op("tensor", "transpose", PT.ap[:, 512 + hp * 128:512 + (hp + 1) * 128],
                            kp[:, hp * 128:(hp + 1) * 128], self.tident_b[:])
                p.waits("vector", [tk] + gget(f"win_rd{par}"))
                e_q = op("vector", "tensor_copy",
                         bass.AP(qTw[par][:].tensor, i * 128, [[4 * 512, 128], [512, 4], [1, 128]]),
                         PT.ap[:, 0:512].rearrange("p (g w) -> p g w", g=4))
                e_k = op("vector", "tensor_copy",
                         bass.AP(kTw[par][:].tensor, i * 128, [[4 * 512, 128], [512, 4], [1, 128]]),
                         PT.ap[:, 512:1024].rearrange("p (g w) -> p g w", g=4))
                PT.rel += [e_q, e_k]
                GG["qp"] = [tq]
                GG["kp"] = [tk]
                kv_rdy += [e_q, e_k]
            GG[f"kv{par}"] = kv_rdy
            GG[f"win_rd{par}"] = []

        def attn_chunk(c, par):
            sp_toks = []
            for hp in range(4):
                ha, hb = 2 * hp, 2 * hp + 1
                pacq(SCa, SCb, AVa, AVb, PV[0])
                for s in range(4):
                    for half, (head, rows, sca, av, tp) in enumerate(
                            ((ha, slice(0, 64), SCa, AVa, (0, 0)),
                             (hb, slice(64, 128), SCb, AVb, (64, 0)))):
                        sl = half * 2 + (s % 2)
                        for jj in range(5):
                            j = s + jj
                            kk = c - 512 + 128 * j
                            gs = lo_q + ((kk - lo_q) // 512) * 512
                            pj = ((kk - lo_q) // 512) % 2
                            tj = (kk - gs) // 128
                            p.waits("tensor", gget(f"kv{pj}") + gget(f"kv{par}"))
                            if jj < 4:
                                out_ap = sca.ap[:, jj * 128:(jj + 1) * 128]
                            else:
                                out_ap = PV[0].ap[:, half * 128:(half + 1) * 128]
                            has_kvb = (c == HALO and j <= 3)
                            has_mask = (jj == 0 or jj == 4)
                            t_s = op("tensor", "matmul", out_ap,
                                     kTw[pj][rows, hp * 512 + tj * 128:hp * 512 + (tj + 1) * 128],
                                     qTw[par][rows, hp * 512 + s * 128:hp * 512 + (s + 1) * 128],
                                     start=True, stop=not (has_kvb or has_mask),
                                     tile_position=tp)
                            if has_kvb:
                                p.waits("tensor", [t_kvb, t_o2b])
                                t_s = op("tensor", "matmul", out_ap,
                                         tkvb2[:, kk:kk + 128],
                                         tones2b[:, 0:128],
                                         start=False, stop=not has_mask)
                            if has_mask:
                                tri = tmtri_up if jj == 0 else tmtri_lo
                                p.waits("tensor", [t_mtu, t_mtl])
                                t_s = op("tensor", "matmul", out_ap,
                                         self.tident_b[:], tri[:],
                                         start=False, stop=True)
                            (sca if jj < 4 else PV[0]).rel.append(t_s)
                            if jj == 3:
                                tsA = t_s
                        tsB = t_s
                        prevav = gget(f"pb{sl}")
                        p.waits("scalar", [tsA, tsB, self.CT["cm8"]] + prevav)
                        t_e1 = op("scalar", "activation", pbuf[:, sl * 640:sl * 640 + 512],
                                  sca.ap[:, 0:512], AF.Exp,
                                  bias=self.consts[:, 0:1], scale=8.0)
                        t_e2 = op("scalar", "activation", pbuf[:, sl * 640 + 512:sl * 640 + 640],
                                  PV[0].ap[:, half * 128:(half + 1) * 128], AF.Exp,
                                  bias=self.consts[:, 0:1], scale=8.0)
                        sca.rel.append(t_e1)
                        PV[0].rel.append(t_e2)
                        p.waits("tensor", [t_e1, t_e2])
                        for jj in range(5):
                            j = s + jj
                            kk = c - 512 + 128 * j
                            gs = lo_q + ((kk - lo_q) // 512) * 512
                            pj = ((kk - lo_q) // 512) % 2
                            tj = (kk - gs) // 128
                            t_a = op("tensor", "matmul",
                                     av.ap[0:72, s * 128:(s + 1) * 128],
                                     sap(vwin[pj], tj * 576 + head * 72, [[1, 72]]),
                                     pbuf[:, sl * 640 + jj * 128:sl * 640 + (jj + 1) * 128],
                                     start=(jj == 0), stop=(jj == 4))
                        av.rel.append(t_a)
                        GG[f"pb{sl}"] = [t_a]
                        last_av = t_a
                p.waits("vector", gget("pb0") + gget("pb1") + gget("pb2") + gget("pb3")
                        + gget("denr"))
                d1 = op("vector", "tensor_copy", denr[64:72, 0:512],
                        AVa.ap[64:72, 0:512])
                AVa.rel.append(d1)
                p.wait("vector", d1)
                d2 = op("vector", "tensor_tensor", denr[64:72, 0:512],
                        denr[64:72, 0:512], AVb.ap[64:72, 0:512], ALU.add)
                AVb.rel.append(d2)
                p.wait("vector", d2)
                d3 = op("vector", "tensor_scalar", denr[64:72, 0:512],
                        denr[64:72, 0:512], 1e-30, None, ALU.max)
                p.wait("vector", d3)
                d4 = op("vector", "reciprocal", denr[64:72, 0:512],
                        denr[64:72, 0:512])
                pass
                pacq(PV[2])
                p.waits("tensor", [d4, t_e8])
                t_rb = op("tensor", "matmul", PV[2].ap[:, 0:512],
                          te8[64:72, hp * 128:(hp + 1) * 128],
                          denr[64:72, 0:512], start=True, stop=True)
                GG["denr"] = [d4, t_rb]
                p.waits("scalar", [t_rb] + gget("rbsb"))
                e_rb = op("scalar", "copy", rbsb[:], PV[2].ap[:, 0:512])
                PV[2].rel.append(e_rb)
                p.waits("vector", [e_rb] + gget(f"ob{hp}"))
                o_a = op("vector", "scalar_tensor_tensor",
                         obuf[0:64, hp * 512:(hp + 1) * 512],
                         AVa.ap[0:64, 0:512], 1.0, rbsb[0:64, 0:512],
                         ALU.mult, ALU.mult)
                p.waits("vector", gget("ostg"))
                o_b = op("vector", "scalar_tensor_tensor", ostg[0:64, 0:512],
                         AVb.ap[0:64, 0:512], 1.0, rbsb[64:128, 0:512],
                         ALU.mult, ALU.mult)
                AVa.rel.append(o_a)
                AVb.rel.append(o_b)
                p.waits("scalar", [o_b])
                o_bd = dma("scalar", obuf[64:128, hp * 512:(hp + 1) * 512],
                           ostg[0:64, 0:512], slot="ob")
                GG["ostg"] = [o_bd]
                GG["rbsb"] = [o_a, o_b]
                GG[f"ob{hp}"] = [o_a, o_bd]
            p.waits("tensor", [GG[f"ob{hp}"][0] for hp in range(4)]
                    + [GG[f"ob{hp}"][1] for hp in range(4)] + [two])
            for mc in range(8):
                pv = PV[mc % 2]
                pacq(pv)
                for kc in range(4):
                    t_w = op("tensor", "matmul", pv.ap[:, 0:512],
                             sap(wo, kc * 1024 + mc * 128, [[1, 128]]),
                             obuf[:, kc * 512:(kc + 1) * 512],
                             start=(kc == 0), stop=(kc == 3))
                p.waits("vector", [t_w] + gget(f"xao{mc % 2}"))
                t_r = op("vector", "scalar_tensor_tensor", xao[mc % 2][:],
                         pv.ap[:, 0:512], 1.0,
                         sap(xfb[par], mc * 513 + 1, [[1, 512]]),
                         ALU.mult, ALU.add)
                pv.rel.append(t_r)
                p.waits("gpsimd", [t_r])
                t_sp = dma("gpsimd", dst_sp[mc * 128:(mc + 1) * 128, c:c + 512],
                           xao[mc % 2][:], slot=f"sp{mc % 4}")
                GG[f"xao{mc % 2}"] = [t_sp]
                sp_toks.append(t_sp)
            GG[f"xfb{par}"] = [t_r]
            for hp in range(4):
                GG[f"ob{hp}"] = GG.get(f"ob{hp}", []) + [t_w]
            GG["win_rd0"] = GG["win_rd1"] = [last_av]
            self.spill_toks = getattr(self, "spill_toks", []) + sp_toks

        groups = list(range(lo_q, R, 512))
        for it, g in enumerate(groups):
            par = it % 2
            toks = load_group(g, par)
            if "Q" not in self.parts:
                qkv_group(g, par, toks)
            if g >= lo_a and "N" not in self.parts:
                attn_chunk(g, par)
            GG[f"x1rd{par}"] = GG.get(f"x1rd_rd{par}", []) \
                + ([] if ln else GG.get(f"xfb{par}", []))
            GG[f"sts{par}"] = GG.get(f"sts_rd{par}", [])
            GG[f"tab{par}"] = GG.get(f"tab_rd{par}", [])
            GG[f"tab_rd{par}"] = []
        self.final_toks = list(getattr(self, "spill_toks", [])) or list(getattr(self, "x1_toks", []))

    def ffn_mlp(self, li, lo_m, src_sp, dst_sp, final):
        p, op, dma, sb = self.p, self.op, self.dma, self.sb
        nc = self.nc
        pfx = f"m{li}_"

        PU = [self.psm(pfx + f"PU{i}", [128, 512], F32) for i in range(4)]
        PD = [self.psm(pfx + f"PD{i}", [128, 512], F32) for i in range(2)]
        PS = self.psm(pfx + "PS", [128, 512], F32)
        PB = self.psm(pfx + "PB", [128, 512], F32)
        RB1, RB2 = PB, PB

        def pacq(*pbs):
            for pb in pbs:
                p.waits("tensor", pb.rel)
                pb.rel = []

        w1 = sb(pfx + "w1", [128, 8 * HID2P], BF16)
        w2 = sb(pfx + "w2", [128, 22 * 1024], BF16)
        tw1 = dma("gpsimd", w1[:].rearrange("p (g n) -> p g n", g=8),
                  self.inp["fw1p"][li].rearrange("(g p) n -> p g n", p=128), slot="w0")
        tw2 = dma("gpsimd", w2[:].rearrange("p (g n) -> p g n", g=22),
                  self.inp["fw2p"][li].rearrange("(g p) n -> p g n", p=128), slot="w1")
        gc = sb(pfx + "gc", [128, 8], F32)
        tgc = dma("sync", gc[:], self.inp["gcol"][li], slot="c1")
        lnb = sb(pfx + "lnb", [128, 8], BF16)
        tlb = dma("sync", lnb[:], self.inp["lnbcb"][li], slot="c2")
        b1c = sb(pfx + "b1c", [128, 44], F32)
        tb1 = dma("sync", b1c[:], self.inp["b1pc"][li], slot="c3")
        b2c = sb(pfx + "b2c", [128, 8], F32)
        tb2 = dma("sync", b2c[:], self.inp["b2col"][li], slot="c0")

        p.waits("vector", [tw1, tgc])
        gfold = []
        for pt in range(8):
            gfold.append(op("vector", "tensor_scalar",
                            sap(w1, pt * HID2P, [[1, HID2P]]),
                            sap(w1, pt * HID2P, [[1, HID2P]]),
                            gc[:, pt:pt + 1], None, ALU.mult))
        ub = sb(pfx + "ub", [128, 44], F32)
        p.waits("tensor", gfold + [tlb, tb1])
        ub_mm = []
        for mc in range(44):
            pv = PU[mc % 2]
            pacq(pv)
            for kc in range(8):
                t_u = op("tensor", "matmul", pv.ap[:, 0:1],
                         sap(w1, kc * HID2P + mc * 128, [[1, 128]]),
                         lnb[:, kc:kc + 1], start=(kc == 0), stop=(kc == 7))
            p.waits("vector", [t_u])
            t_ub = op("vector", "tensor_tensor", ub[:, mc:mc + 1], pv.ap[:, 0:1],
                      b1c[:, mc:mc + 1], ALU.add)
            pv.rel.append(t_ub)
            ub_mm.append(t_ub)

        xab = sb(pfx + "xab", [128, 8 * 513], F32)
        xbb = sb(pfx + "xbb", [128, 8 * 513], BF16)
        xsq = sb(pfx + "xsq", [128, 8 * 513], BF16)
        h3 = sb(pfx + "h3", [128, 22 * 512], BF16)
        gel = sb(pfx + "gel", [128, 512], BF16)
        rows = sb(pfx + "rows", [1, 5 * 512], F32)
        st2 = sb(pfx + "st2", [2, 512], F32)
        rbs2 = sb(pfx + "rbs2", [128, 512], F32)
        mrsb = sb(pfx + "mrsb", [128, 512], F32)
        xlo = [sb(pfx + "xlo0", [128, 512], F32)] * 2

        GG = {}

        def gget(k):
            return GG.setdefault(k, [])

        if lo_m % 512 != 0:
            nxt = (lo_m // 512 + 1) * 512
            groups = [(lo_m, nxt)] + _ranges(nxt, R)
        else:
            groups = _ranges(lo_m, R)
        for gi, (a, b) in enumerate(groups):
            W = b - a
            a3 = [[513, 8], [1, W + 1]]
            p.waits("sync", gget("xab"))
            t_x = dma("sync", xab[:].rearrange("p (gg w) -> p gg w", gg=8)[:, :, 0:W + 1],
                      src_sp[:, a - 1:b].rearrange("(gg p) w -> p gg w", p=128),
                      slot="x0")
            mask_tok = t_x
            if a == HALO:
                p.waits("vector", [t_x, self.CT["shiftc"]])
                mask_tok = op("vector", "tensor_scalar",
                              sap(xab, 4 * 513, [[513, 4], [1, 1]]),
                              sap(xab, 4 * 513, [[513, 4], [1, 1]]),
                              self.tshiftc[:, 0:1], None, ALU.mult)

            p.waits("vector", [mask_tok] + gget("xbb"))
            c1 = op("vector", "tensor_copy", sap(xbb, 0, a3), sap(xab, 0, a3))
            p.waits("scalar", [c1] + gget("xsq"))
            c2 = op("scalar", "activation", sap(xsq, 0, a3), sap(xbb, 0, a3), AF.Square)
            pacq(PS, PD[0])
            p.waits("tensor", [c1, c2, self.CT["onesbf"]])
            t_st = None
            for (pb, base), src, cl, fh in (((PS, 0), xbb, 1, True), ((PS, 32), xbb, 0, False),
                                            ((PS, 64), xsq, 1, True), ((PD[0], 0), xsq, 0, False)):
                for h4 in range(4):
                    pt = (0 if fh else 4) + h4
                    t_st = op("tensor", "matmul", pb.ap[base:base + 1, 0:W],
                              self.tonesbf[:],
                              sap(src, pt * 513 + cl, [[1, W]]),
                              start=(h4 == 0), stop=(h4 == 3))
            sg = lambda j: rows[0:1, j * 512:j * 512 + W]
            p.waits("scalar", [t_st] + gget("rows"))
            te = op("scalar", "copy", sg(0), PS.ap[0:1, 0:W])
            te2 = op("scalar", "copy", sg(1), PS.ap[32:33, 0:W])
            te3 = op("scalar", "copy", sg(2), PS.ap[64:65, 0:W])
            te4 = op("scalar", "copy", sg(3), PD[0].ap[0:1, 0:W])
            PS.rel += [te, te2, te3]
            PD[0].rel.append(te4)
            p.waits("vector", [te, te2, te3, te4])
            r1 = op("vector", "tensor_add", sg(4), sg(0), sg(1))
            p.wait("vector", r1)
            r1b = op("vector", "tensor_scalar", sg(4), sg(4), 1.0 / 1024.0, None, ALU.mult)
            r2 = op("vector", "tensor_add", sg(0), sg(2), sg(3))
            p.waits("vector", [r1b, r2])
            r2b = op("vector", "tensor_scalar", sg(0), sg(0), 1.0 / 1024.0, None, ALU.mult)
            r3 = op("vector", "tensor_tensor", sg(1), sg(4), sg(4), ALU.mult)
            p.waits("vector", [r2b, r3])
            r4 = op("vector", "tensor_tensor", sg(1), sg(0), sg(1), ALU.subtract)
            p.wait("vector", r4)
            r5 = op("vector", "tensor_scalar", sg(1), sg(1), 1.0, 1e-5, ALU.mult, ALU.add)
            p.wait("scalar", r5)
            r6 = op("scalar", "activation", sg(2), sg(1), AF.Sqrt)
            p.wait("vector", r6)
            r7 = op("vector", "reciprocal", sg(3), sg(2))
            p.wait("vector", r7)
            r8 = self.newton(sg(2), sg(1), sg(3))
            p.wait("vector", r8)
            r9 = op("vector", "tensor_tensor", sg(4), sg(4), sg(3), ALU.mult)
            p.wait("vector", r9)
            r10 = op("vector", "tensor_scalar", sg(4), sg(4), -1.0, None, ALU.mult)
            p.waits("gpsimd", [r8, r10] + gget("st2"))
            c_r = dma("gpsimd", st2[0:1, 0:W], sg(3), slot="str")
            c_m = dma("gpsimd", st2[1:2, 0:W], sg(4), slot="stm")
            GG["rows"] = [c_r, c_m]
            pacq(PB)
            p.waits("tensor", [c_r, c_m, self.CT["onesZa"], self.CT["onesZb"]])
            t_rb = op("tensor", "matmul", PB.ap[:, 0:W], self.tonesZa[:],
                      st2[:, 0:W], start=True, stop=True)
            p.waits("scalar", [t_rb] + gget("rbs2"))
            e1 = op("scalar", "copy", rbs2[:, 0:W], PB.ap[:, 0:W])
            PB.rel.append(e1)
            pacq(PB)
            t_mb = op("tensor", "matmul", PB.ap[:, 0:W], self.tonesZb[:],
                      st2[:, 0:W], start=True, stop=True)
            p.waits("scalar", [t_mb] + gget("rbs2m"))
            e2 = op("scalar", "copy", mrsb[:, 0:W], PB.ap[:, 0:W])
            PB.rel.append(e2)
            GG["st2"] = [t_rb, t_mb]

            p.waits("gpsimd", [e1, mask_tok, t_st] + gget("h2g"))
            a4f = [[513, 4], [1, W]]
            gh1 = op("gpsimd", "tensor_tensor", sap(xsq, 0 * 513, a4f),
                     sap(xab, 0 * 513 + 1, a4f),
                     bass.AP(rbs2[:].tensor, 0, [[512, 128], [0, 4], [1, W]]), ALU.mult)
            gh2 = op("gpsimd", "tensor_tensor", sap(xsq, 4 * 513, a4f),
                     sap(xab, 4 * 513 + 0, a4f),
                     bass.AP(rbs2[:].tensor, 0, [[512, 128], [0, 4], [1, W]]), ALU.mult)
            GG["rbs2"] = [gh1, gh2]
            GG["rbs2m"] = []
            p.waits("vector", [gh1, gh2, e2, t_st] + gget("h2v"))
            vh1 = op("vector", "tensor_tensor", sap(xbb, 0 * 513, a4f),
                     sap(xsq, 0 * 513, a4f),
                     bass.AP(mrsb[:].tensor, 0, [[512, 128], [0, 4], [1, W]]), ALU.add)
            vh2 = op("vector", "tensor_tensor", sap(xbb, 4 * 513, a4f),
                     sap(xsq, 4 * 513, a4f),
                     bass.AP(mrsb[:].tensor, 0, [[512, 128], [0, 4], [1, W]]), ALU.add)
            GG["h2g"] = [vh1, vh2]
            GG["xsq"] = [vh1, vh2]
            GG["h2v"] = []
            GG["xbb"] = []
            h2_tok = [vh1, vh2]

            h3_toks = []
            p.waits("tensor", h2_tok + [tw2] + ub_mm)
            for sl in range(NPAIR):
                pa = [PU[0], PU[1]]
                pg = [PU[2], PU[3]]
                for half in range(2):
                    pacq(pa[half], pg[half])
                    for kc in range(8):
                        t_a = op("tensor", "matmul", pa[half].ap[:, 0:W],
                                 sap(w1, kc * HID2P + sl * 512 + half * 128, [[1, 128]]),
                                 sap(xbb, kc * 513, [[1, W]]),
                                 start=(kc == 0), stop=(kc == 7))
                    for kc in range(8):
                        t_g = op("tensor", "matmul", pg[half].ap[:, 0:W],
                                 sap(w1, kc * HID2P + sl * 512 + 256 + half * 128, [[1, 128]]),
                                 sap(xbb, kc * 513, [[1, W]]),
                                 start=(kc == 0), stop=(kc == 7))
                    mc_a = sl * 4 + half
                    mc_g = sl * 4 + 2 + half
                    p.waits("scalar", [t_g] + gget("gel"))
                    s_g = op("scalar", "activation", gel[:, 0:W], pg[half].ap[:, 0:W],
                             AF.Gelu, bias=ub[:, mc_g:mc_g + 1])
                    pg[half].rel.append(s_g)
                    p.waits("vector", [s_g, t_a] + gget(f"h3_{mc_a}"))
                    v_h3 = op("vector", "scalar_tensor_tensor",
                              h3[:, (sl * 2 + half) * 512:(sl * 2 + half) * 512 + W],
                              pa[half].ap[:, 0:W], ub[:, mc_a:mc_a + 1], gel[:, 0:W],
                              ALU.add, ALU.mult)
                    pa[half].rel.append(v_h3)
                    GG["gel"] = [v_h3]
                    GG[f"h3_{mc_a}"] = []
                    h3_toks.append(v_h3)
            GG["xbb"] = [t_a, t_g]

            p.waits("tensor", h3_toks + [tw2])
            res_toks = []
            for mc in range(8):
                pv = PD[mc % 2]
                pacq(pv)
                for kc in range(22):
                    t_w = op("tensor", "matmul", pv.ap[:, 0:W],
                             sap(w2, kc * 1024 + mc * 128, [[1, 128]]),
                             h3[:, kc * 512:kc * 512 + W],
                             start=(kc == 0), stop=(kc == 21))
                p.waits("vector", [t_w, tb2] + gget("xlo"))
                t_r = op("vector", "scalar_tensor_tensor", xlo[mc % 2][:, 0:W],
                         pv.ap[:, 0:W], b2c[:, mc:mc + 1],
                         sap(xab, mc * 513 + 1, [[1, W]]), ALU.add, ALU.add)
                pv.rel.append(t_r)
                p.waits("gpsimd", [t_r])
                if final:
                    yslice = self.y[mc * 128:(mc + 1) * 128, a - HALO:b - HALO]
                    t_c = dma("gpsimd", yslice,
                              self.x1sp[mc * 128:(mc + 1) * 128, a:b],
                              slot=f"sp{mc % 4}")
                    t_sp = dma("gpsimd", yslice, xlo[mc % 2][:, 0:W],
                               slot=f"sp{mc % 4}", accum=ALU.add)
                else:
                    t_sp = dma("gpsimd", dst_sp[mc * 128:(mc + 1) * 128, a:b],
                               xlo[mc % 2][:, 0:W], slot=f"sp{mc % 4}")
                GG["xlo"] = [t_sp]
                res_toks.append(t_sp)
            GG["xab"] = [t_r, gh1, gh2, t_sp] + h2_tok
            self.mlp_toks = getattr(self, "mlp_toks", []) + res_toks
        self.final_toks = list(self.mlp_toks)



def host_prepare(inputs):
    X = np.ascontiguousarray(np.asarray(inputs["L0_attn_x"], np.float32))
    W_qkv = np.ascontiguousarray(np.asarray(inputs["L0_attn_W_qkv"], np.float32))
    WoutT = np.ascontiguousarray(np.asarray(inputs["L0_attn_W_out"], np.float32).T)
    b_out = np.asarray(inputs["L0_attn_b_out"], np.float32)
    fwqkv = np.ascontiguousarray(np.asarray(inputs["L0_ffn_wqkv"], np.float32))
    fqs = np.asarray(inputs["L0_ffn_q_scale"], np.float32)
    fks = np.asarray(inputs["L0_ffn_k_scale"], np.float32)
    fwo = np.ascontiguousarray(np.asarray(inputs["L0_ffn_wo"], np.float32))
    flng = np.asarray(inputs["L0_ffn_ln_g"], np.float32)
    flnb = np.asarray(inputs["L0_ffn_ln_b"], np.float32)
    fw1 = np.asarray(inputs["L0_ffn_w1"], np.float32)
    fb1 = np.asarray(inputs["L0_ffn_b1"], np.float32)
    fw2 = np.asarray(inputs["L0_ffn_w2"], np.float32)
    fb2 = np.asarray(inputs["L0_ffn_b2"], np.float32)

    w1p = np.zeros((2, D, HID2P), np.float32)
    b1p = np.zeros((2, HID2P), np.float32)
    for j in range(NPAIR):
        lo = j * 256
        wa = min(256, HID - lo)
        w1p[:, :, j * 512:j * 512 + wa] = fw1[:, :, lo:lo + wa]
        w1p[:, :, j * 512 + 256:j * 512 + 256 + wa] = fw1[:, :, HID + lo:HID + lo + wa]
        b1p[:, j * 512:j * 512 + wa] = fb1[:, lo:lo + wa]
        b1p[:, j * 512 + 256:j * 512 + 256 + wa] = fb1[:, HID + lo:HID + lo + wa]
    w2p = np.zeros((2, HIDP, D), np.float32)
    for j in range(NPAIR):
        lo = j * 256
        wa = min(256, HID - lo)
        w2p[:, j * 256:j * 256 + wa, :] = fw2[:, lo:lo + wa, :]
    b1pc = np.ascontiguousarray(b1p.reshape(2, 44, 128).transpose(0, 2, 1))
    gcol = np.ascontiguousarray(flng.reshape(2, 8, 128).transpose(0, 2, 1))
    lnbcb = np.ascontiguousarray(flnb.reshape(2, 8, 128).transpose(0, 2, 1)).astype(BF)
    b2col = np.ascontiguousarray(fb2.reshape(2, 8, 128).transpose(0, 2, 1))
    boutc = np.ascontiguousarray(b_out.reshape(8, 128).T)
    qsrep = np.ascontiguousarray(
        np.broadcast_to(np.tile(fqs, (1, 8))[:, None, :], (2, 128, 512)))
    ksrep = np.ascontiguousarray(
        np.broadcast_to(np.tile(fks, (1, 8))[:, None, :], (2, 128, 512)))

    ii = np.arange(128)
    identb = np.eye(128, dtype=np.float32).astype(BF)
    identf = np.eye(128, dtype=np.float32)
    mcaus = np.where(ii[:, None] >= ii[None, :], 0.0, NEG).astype(BF)
    mtri_up = np.where(ii[None, :] <= ii[:, None], 0.0, NEG).astype(BF)
    mtri_lo = np.where(ii[None, :] >= ii[:, None], 0.0, NEG).astype(BF)
    onesZa = np.zeros((2, 128), np.float32)
    onesZa[0] = 1.0
    onesZb = np.zeros((2, 128), np.float32)
    onesZb[1] = 1.0
    e8 = np.zeros((128, 4 * 128), np.float32)
    for pg in range(4):
        for pp in range(128):
            e8[64 + 2 * pg + pp // 64, pg * 128 + pp] = 1.0
    onesbf = np.ones((128, 1), np.float32).astype(BF)
    ones2bv = np.zeros((2, 512), np.float32)
    ones2bv[0] = 1.0
    ones2bv = ones2bv.astype(BF)

    shared = dict(Wqkv=W_qkv, WoutT=WoutT, boutc=boutc, fwqkv=fwqkv, fwo=fwo,
                  qsrep=qsrep, ksrep=ksrep, gcol=gcol, lnbcb=lnbcb,
                  fw1p=w1p, b1pc=b1pc, fw2p=w2p, b2col=b2col,
                  mtri_up=np.ascontiguousarray(mtri_up),
                  mtri_lo=np.ascontiguousarray(mtri_lo),
                  mcaus=np.ascontiguousarray(mcaus),
                  identb=np.ascontiguousarray(identb), identf=identf,
                  onesZa=onesZa, onesZb=onesZb, e8=e8, onesbf=onesbf, ones2b=ones2bv)

    def rot_tables(glob_of):
        half = np.arange(0, 64, 2, dtype=np.float64)
        inv_freq = 1.0 / 10000.0 ** (half / 64)
        t = np.arange(glob_of, glob_of + R, dtype=np.float64)
        fr = t[:, None] * inv_freq[None, :]
        fr = np.concatenate([fr, fr], -1)
        base = (half + 0.4 * 64) / (1.4 * 64)
        power = (t - 4096.0) / 256.0
        sc = base[None, :] ** power[:, None]
        sc = np.concatenate([sc, sc], -1)
        cos, sin = np.cos(fr), np.sin(fr)
        sgn = np.ones(64)
        sgn[:32] = -1.0
        return ((cos * sc).astype(np.float32),
                (sin * sc * sgn[None, :]).astype(np.float32),
                (cos / sc).astype(np.float32),
                (sin / sc * sgn[None, :]).astype(np.float32))

    in_maps = []
    for c in range(8):
        b, h = c // 2, c % 2
        own0 = h * OWN
        lo = own0 - HALO
        xcore = np.zeros((R, D), np.float32)
        s = max(lo, 0)
        xcore[s - lo:] = X[b, s:own0 + OWN]
        kvb2 = np.zeros((2, R), np.float32)
        if h == 0:
            kvb2[0, :HALO] = NEG
        shiftc = np.full((128, 1), 0.0 if h == 0 else 1.0, np.float32)
        cq, sq, ck, sk = rot_tables(lo)
        m = dict(shared)
        m.update(xin=np.ascontiguousarray(xcore), kvb2=kvb2.astype(BF),
                 shiftc=shiftc, cq=cq, sq=sq, ck=ck, sk=sk)
        in_maps.append(m)
    return in_maps


def kernel(**inputs):
    in_maps = host_prepare(inputs)
    mk = MK(debug=False, parts="ABCDEVST")
    nc = mk.build()
    res = run_bass_kernel_spmd(nc, in_maps, core_ids=list(range(8)))
    out = np.zeros((4, 8192, 1024), np.float32)
    for c in range(8):
        b, h = c // 2, c % 2
        out[b, h * OWN:(h + 1) * OWN] = np.asarray(res.results[c]["y"]).T
    return out
